# version 10
# baseline (speedup 1.0000x reference)
import contextlib
import math

import numpy as np
import ml_dtypes
import concourse.bass as bass
import concourse.mybir as mybir
from concourse.bass_utils import run_bass_kernel_spmd

F32 = mybir.dt.float32
BF16 = mybir.dt.bfloat16
AF = mybir.ActivationFunctionType
ALU = mybir.AluOpType
AX = mybir.AxisListType

NCORES = 8
S = 2048
D = 1024
NT = 4
TB = 512
H = 16
LN_EPS = 1e-5
PIPE_K = 2
BF = ml_dtypes.bfloat16


def arena_of(key):
    return key.split(":", 1)[0]


class Prog:
    ENGINES = ("pe", "act", "dve", "pool", "sp")

    def __init__(self, nc, es):
        self.nc = nc
        self.q = {e: [] for e in self.ENGINES}
        self.cnt = {e: 0 for e in self.ENGINES}
        self.sem = {e: es.enter_context(nc.semaphore("sem_" + e)) for e in self.ENGINES}
        self.es = es
        self.dsem = {}
        self.dcnt = {}
        self.wm = {e: {} for e in self.ENGINES}
        self.reg = {}
        self.pending = {}

    def dma_sem(self, group):
        if group not in self.dsem:
            self.dsem[group] = self.es.enter_context(self.nc.semaphore("dma_" + group))
            self.dcnt[group] = 0
        return self.dsem[group]

    def _collect(self, waits, e, semkey, val, own):
        if semkey == ("e", e):
            own[0] = max(own[0], val)
            return
        if val > self.wm[e].get(semkey, 0):
            waits[semkey] = max(waits.get(semkey, 0), val)

    def emit(self, e, fn, reads=(), writes=(), dma=None):
        waits = {}
        own = [0]
        seen_ar = set()
        for r in reads:
            ar = arena_of(r)
            if ar not in seen_ar and ar in self.pending:
                seen_ar.add(ar)
                for k, v in self.pending[ar]:
                    self._collect(waits, e, k, v, own)
            st = self.reg.get(r)
            if st and st["w"] is not None:
                self._collect(waits, e, *st["w"], own)
        for w in writes:
            ar = arena_of(w)
            if ar not in seen_ar and ar in self.pending:
                seen_ar.add(ar)
                for k, v in self.pending[ar]:
                    self._collect(waits, e, k, v, own)
            st = self.reg.get(w)
            if st:
                if st["w"] is not None:
                    self._collect(waits, e, *st["w"], own)
                for k, v in st["r"]:
                    self._collect(waits, e, k, v, own)
        if dma is not None:
            self.dma_sem(dma)
            self.dcnt[dma] += 16
            my = (("d", dma), self.dcnt[dma])
        else:
            self.cnt[e] += 1
            my = (("e", e), self.cnt[e])
        if own[0] > self.wm[e].get(("e", e), 0) and (e != "pe" or dma is not None):
            waits[("e", e)] = own[0]
        waits_l = []
        for (kind, key), val in sorted(waits.items(), key=lambda kv: str(kv[0])):
            h = self.sem[key] if kind == "e" else self.dsem[key]
            waits_l.append((h, val, ((kind, key), val)))
        myhandle = self.sem[e] if dma is None else self.dsem[dma]
        inc = 16 if dma is not None else 1

        def closure(h, waits_l=waits_l, fn=fn, myhandle=myhandle, inc=inc):
            for hsem, val, _ in waits_l:
                h.wait_ge(hsem, val)
            fn(h).then_inc(myhandle, inc)

        self.q[e].append(closure)
        for _, val, (k, v) in waits_l:
            self.wm[e][k] = max(self.wm[e].get(k, 0), v)
        for r in reads:
            st = self.reg.setdefault(r, {"w": None, "r": []})
            st["r"].append(my)
            if len(st["r"]) > 24:
                st["r"] = st["r"][-24:]
        for w in writes:
            self.reg[w] = {"w": my, "r": []}
        return my

    def merge(self, prefix):
        acc = dict(self.pending.get(prefix, []))
        for k in list(self.reg):
            if arena_of(k) == prefix:
                st = self.reg.pop(k)
                items = ([st["w"]] if st["w"] is not None else []) + st["r"]
                for kk, vv in items:
                    acc[kk] = max(acc.get(kk, 0), vv)
        self.pending[prefix] = list(acc.items())

    def finalize(self):
        with self.nc.Block() as blk:
            for e in self.ENGINES:
                qe = self.q[e]
                if not qe:
                    continue
                sec = {"pe": blk.tensor, "act": blk.scalar, "dve": blk.vector,
                       "pool": blk.gpsimd, "sp": blk.sync}[e]

                def body(h, qe=qe):
                    for f in qe:
                        f(h)
                sec(body)


class Ctx:
    pass


def k_ps(i):
    return "ps%d" % i


def alloc_common(nc, es, io_names):
    c = Ctx()
    c.nc = nc
    c.p = Prog(nc, es)
    c.dram = {}
    for name, (shape, dt, kind) in io_names.items():
        c.dram[name] = nc.dram_tensor(name, list(shape), dt, kind=kind).ap()
    sb = lambda n, shape, dt: es.enter_context(nc.sbuf_tensor(n, shape, dt))
    c.xT = sb("xT", [128, 8 * S], F32)
    c.AR1 = sb("a1", [128, 8 * S], BF16)
    c.AR2 = sb("a2", [128, 8 * S], BF16)
    c.AR3 = sb("a3", [128, 8 * S], BF16)
    c.AR4 = sb("a4", [128, 16640], BF16)
    c.WS = sb("ws", [128, 2048], BF16)
    c.SQ = sb("sq", [128, 1024], BF16)
    c.ROWS = sb("rows", [128, 1536], F32)
    c.BROWS = sb("brows", [128, 1024], BF16)
    c.CN = sb("cn", [128, 432], BF16)
    c.CNF = sb("cnf", [128, 128], F32)
    c.ps = [es.enter_context(nc.psum_tensor("ps%d" % i, [128, TB], F32))
            for i in range(8)]
    return c




def emit_init_consts(c, need_ident=False):
    p = c.p
    p.emit("dve", lambda v: v.memset(c.CN[:, 0:1], 1.0), writes=["cn:onescol"])
    p.emit("dve", lambda v: v.memset(c.CN[0:1, 2:130], 1.0), writes=["cn:onesrow"])
    if need_ident:
        p.emit("pool", lambda g: g.dma_start(
            out=c.CN[:, 130:258], in_=c.dram["ident"][:, :]),
            writes=["cn:ident"], dma="w")


def emit_load_xT(c, src="x_in"):
    p = c.p
    x = c.dram[src]
    for cc in range(8):
        p.emit("sp", lambda sy, cc=cc: sy.dma_start(
            out=c.xT[:, cc * S:(cc + 1) * S], in_=x[cc * 128:(cc + 1) * 128, :]),
            writes=["xT:%d" % cc], dma="x")


def emit_store_xT(c, dst="x_out"):
    p = c.p
    x = c.dram[dst]
    for cc in range(8):
        p.emit("sp", lambda sy, cc=cc: sy.dma_start(
            out=x[cc * 128:(cc + 1) * 128, :], in_=c.xT[:, cc * S:(cc + 1) * S]),
            reads=["xT:%d" % cc], dma="xst")


def emit_final_wait(c):
    p = c.p
    total = p.dcnt.get("xst", 0)
    if total:
        def closure(h, sem=p.dsem["xst"], val=total):
            h.wait_ge(sem, val)
        p.q["sp"].append(closure)


def emit_layernorm(c, dst, dst_key, src=None, src_key=None, gains=None,
                   inplace=False):
    p = c.p
    from_f32 = src is None
    if from_f32:
        src, src_key = c.xT, "xT"
    r_mu, r_tmp, r_rstd = (c.ROWS[0:1, 0:512], c.ROWS[0:1, 512:1024],
                           c.ROWS[0:1, 1024:1536])
    bmu = brst = c.BROWS[0:1, 0:512]
    onescol, onesrow = c.CN[:, 0:1], c.CN[0:1, 2:130]
    epsc = c.CNF[0:1, 127:128]
    p.emit("dve", lambda v: v.memset(epsc, float(LN_EPS)), writes=["cnf:eps"])
    for nb in range(NT):
        for cc in range(8):
            off = cc * S + nb * TB
            ssl = src[:, off:off + TB]
            skey = "%s:%d" % (src_key, cc)
            sqs = c.SQ[:, (cc % 2) * TB:(cc % 2 + 1) * TB]
            sqk = "sq:%d" % (cc % 2)
            if from_f32:
                wsl = c.WS[:, (cc % 2) * 1024:(cc % 2) * 1024 + TB]
                p.emit("dve", lambda v, ssl=ssl, wsl=wsl: v.tensor_copy(wsl, ssl),
                       reads=[skey], writes=["ws:%d" % (cc % 2)])
                opnd, okey = wsl, "ws:%d" % (cc % 2)
            else:
                opnd, okey = ssl, skey
            p.emit("pe", lambda t, opnd=opnd, cc=cc: t.matmul(
                c.ps[6][0:1, :], onescol, opnd, start=(cc == 0), stop=(cc == 7)),
                reads=[okey, "cn:onescol"], writes=[k_ps(6)])
            p.emit("act", lambda s, ssl=ssl, sqs=sqs: s.activation(sqs, ssl, AF.Square),
                   reads=[skey], writes=[sqk])
            p.emit("pe", lambda t, cc=cc, sqs=sqs: t.matmul(
                c.ps[7][0:1, :], onescol, sqs, start=(cc == 0), stop=(cc == 7)),
                reads=[sqk, "cn:onescol"], writes=[k_ps(7)])
        p.emit("dve", lambda v: v.tensor_scalar(
            r_mu, c.ps[6][0:1, :], 1.0 / D, None, ALU.mult),
            reads=[k_ps(6)], writes=["rows:mu"])
        p.emit("dve", lambda v: v.tensor_scalar(
            r_tmp, c.ps[7][0:1, :], 1.0 / D, None, ALU.mult),
            reads=[k_ps(7)], writes=["rows:tmp"])
        p.emit("dve", lambda v: v.tensor_tensor(r_rstd, r_mu, r_mu, ALU.mult),
               reads=["rows:mu"], writes=["rows:rstd"])
        p.emit("dve", lambda v: v.tensor_tensor(r_tmp, r_tmp, r_rstd, ALU.subtract),
               reads=["rows:tmp", "rows:rstd"], writes=["rows:tmp"])
        p.emit("act", lambda s: s.activation(r_rstd, r_tmp, AF.Sqrt, bias=epsc),
               reads=["rows:tmp", "cnf:eps"], writes=["rows:rstd"])
        p.emit("dve", lambda v: v.reciprocal(r_tmp, r_rstd),
               reads=["rows:rstd"], writes=["rows:tmp"])
        p.emit("dve", lambda v: v.tensor_copy(bmu, r_mu),
               reads=["rows:mu"], writes=["brows:0"])
        p.emit("pe", lambda t: t.matmul(c.ps[4][:, :], onesrow, bmu, start=True, stop=True),
               reads=["brows:0", "cn:onesrow"], writes=[k_ps(4)])
        p.emit("dve", lambda v: v.tensor_copy(brst, r_tmp),
               reads=["rows:tmp"], writes=["brows:0"])
        p.emit("pe", lambda t: t.matmul(c.ps[5][:, :], onesrow, brst, start=True, stop=True),
               reads=["brows:0", "cn:onesrow"], writes=[k_ps(5)])
        for cc in range(8):
            off = cc * S + nb * TB
            ssl = src[:, off:off + TB]
            dsl = dst[:, off:off + TB]
            skey = "%s:%d" % (src_key, cc)
            dkey = "%s:%d" % (dst_key, cc)
            wr = [dkey, skey] if inplace else [dkey]
            p.emit("dve", lambda v, ssl=ssl, dsl=dsl: v.tensor_tensor(
                dsl, ssl, c.ps[4][:, :], ALU.subtract),
                reads=[skey, k_ps(4)], writes=wr)
            p.emit("dve", lambda v, dsl=dsl: v.tensor_tensor(
                dsl, dsl, c.ps[5][:, :], ALU.mult),
                reads=[dkey, k_ps(5)], writes=[dkey])
            if gains is not None:
                g8, b8, gkey = gains
                p.emit("dve", lambda v, dsl=dsl, cc=cc: v.tensor_scalar(
                    dsl, dsl, g8[:, cc:cc + 1], b8[:, cc:cc + 1], ALU.mult, ALU.add),
                    reads=[dkey, gkey], writes=[dkey])


def emit_projA(c, wdram, n_out, src, src_key, sink,
               banks=((0, 1, 2, 3), (4, 5, 6, 7))):
    p = c.p
    for ot in range(n_out // 128):
        slot = ot % 2
        wk = "ws:%d" % slot
        wv3 = c.WS[:, slot * 1024:(slot + 1) * 1024].rearrange(
            "p (kc n) -> p kc n", kc=8)
        p.emit("pool", lambda g, ot=ot, wv3=wv3: g.dma_start(
            out=wv3, in_=wdram[:, ot * 128:(ot + 1) * 128]
            .rearrange("(kc p) n -> p kc n", p=128)),
            writes=[wk], dma="w")
        bks = banks[ot % 2]

        def mmg(t, wv3=wv3, bks=bks):
            last = None
            for kc in range(8):
                for nb in range(NT):
                    last = t.matmul(
                        c.ps[bks[nb]][:, :], wv3[:, kc, :],
                        src[:, kc * S + nb * TB: kc * S + (nb + 1) * TB],
                        start=(kc == 0), stop=(kc == 7))
            return last
        p.emit("pe", mmg,
               reads=[wk] + ["%s:%d" % (src_key, kc) for kc in range(8)],
               writes=[k_ps(b) for b in bks])
        for nb in range(NT):
            sink(ot, nb, bks[nb], k_ps(bks[nb]))


def alibi_slopes(n):
    start = 2.0 ** (-(2.0 ** -(math.log2(n) - 3)))
    return [start * (start ** i) for i in range(n)]


def emit_attention(c, uid, kind, wq, wk, wv, wo, bo_cols=None, masks_name=None):
    p = c.p
    scale = 0.125
    q_ar, k_ar = c.AR2, c.AR3
    p.merge("a1"); p.merge("a2"); p.merge("a3"); p.merge("a4"); p.merge("cnf")
    if bo_cols is not None:
        p.emit("sp", lambda sy: sy.dma_start(
            out=c.CNF[:, 72:80], in_=c.dram[bo_cols][:, :]),
            writes=["cnf:bo"], dma="x")
    emit_layernorm(c, c.AR1, "a1", src_key=None)
    xn, xnk = c.AR1, "a1"

    def mk_sink(arena, keyp):
        def sink(ot, nb, bank, pk):
            dsl = arena[:, ot * S + nb * TB: ot * S + (nb + 1) * TB]
            p.emit("act", lambda s, dsl=dsl, bank=bank: s.activation(
                dsl, c.ps[bank][:, :], AF.Copy),
                reads=[pk], writes=["%s:%d" % (keyp, ot)])
        return sink
    emit_projA(c, c.dram[wq], D, xn, xnk, mk_sink(q_ar, "a2"))
    emit_projA(c, c.dram[wk], D, xn, xnk, mk_sink(k_ar, "a3"))

    vt = c.AR4.rearrange("p (jt ch) -> p jt ch", jt=16)
    p.emit("dve", lambda v: v.memset(
        vt.rearrange("p jt (hh cc) -> p jt hh cc", cc=65)[:, :, :, 64:65], 1.0),
        writes=["a4:ones"])
    vstage = c.SQ[:, 0:512]

    def v_sink(ot, nb, bank, pk):
        p.emit("act", lambda s, bank=bank: s.activation(
            vstage, c.ps[bank][:, :], AF.Copy), reads=[pk], writes=["sq:0"])
        for tt in range(4):
            jt = nb * 4 + tt
            pb = c.ps[3].bitcast(BF16)
            p.emit("pe", lambda t, tt=tt: t.transpose(
                pb[:, 0:128], vstage[:, tt * 128:(tt + 1) * 128], c.CN[:, 130:258]),
                reads=["sq:0", "cn:ident"], writes=[k_ps(3)])
            dst = vt[:, jt, :].rearrange("p (hh cc) -> p hh cc", cc=65)[
                :, 2 * ot:2 * ot + 2, 0:64]
            srcv = pb[:, 0:128].rearrange("p (hh cc) -> p hh cc", cc=64)
            p.emit("act", lambda s, dst=dst, srcv=srcv: s.activation(dst, srcv, AF.Copy),
                   reads=[k_ps(3)], writes=["a4:v:%d:%d" % (jt, ot)])
    emit_projA(c, c.dram[wv], D, xn, xnk, v_sink,
               banks=((0, 1, 2, 4), (5, 6, 7, 2)))

    def norms(keyp, arena, colbase):
        for pt in range(8):
            for nb in range(NT):
                asl = arena[:, pt * S + nb * TB: pt * S + (nb + 1) * TB]
                sqs = c.SQ[:, (nb % 2) * TB:(nb % 2 + 1) * TB]
                sqk = "sq:%d" % (nb % 2)
                p.emit("act", lambda s, asl=asl, sqs=sqs: s.activation(
                    sqs, asl, AF.Square), reads=["%s:%d" % (keyp, pt)], writes=[sqk])
                p.emit("pe", lambda t, sqs=sqs: t.matmul(
                    c.ps[6][0:1, :], c.CN[:, 0:1], sqs, start=True, stop=True),
                    reads=[sqk, "cn:onescol"], writes=[k_ps(6)])
                col = colbase + pt * NT + nb
                p.emit("dve", lambda v, col=col: v.reduce_max(
                    c.CNF[0:1, col:col + 1], c.ps[6][0:1, :], axis=AX.X),
                    reads=[k_ps(6)], writes=["cnf:nm%d" % col])
    norms("a2", q_ar, 0)
    norms("a3", k_ar, 32)
    rd = ["cnf:nm%d" % i for i in range(64)]
    p.emit("dve", lambda v: v.reduce_max(c.CNF[0:1, 64:65], c.CNF[0:1, 0:32], axis=AX.X),
           reads=rd[:32], writes=["cnf:mq2"])
    p.emit("dve", lambda v: v.reduce_max(c.CNF[0:1, 65:66], c.CNF[0:1, 32:64], axis=AX.X),
           reads=rd[32:], writes=["cnf:mk2"])
    p.emit("dve", lambda v: v.tensor_tensor(
        c.CNF[0:1, 66:67], c.CNF[0:1, 64:65], c.CNF[0:1, 65:66], ALU.mult),
        reads=["cnf:mq2", "cnf:mk2"], writes=["cnf:mprod"])
    p.emit("act", lambda s: s.activation(c.CNF[0:1, 67:68], c.CNF[0:1, 66:67], AF.Sqrt),
           reads=["cnf:mprod"], writes=["cnf:msqrt"])
    p.emit("dve", lambda v: v.tensor_scalar(
        c.CNF[0:1, 68:69], c.CNF[0:1, 67:68], -scale, None, ALU.mult),
        reads=["cnf:msqrt"], writes=["cnf:mneg"])
    p.emit("dve", lambda v: v.tensor_copy(c.CN[0:1, 286:287], c.CNF[0:1, 68:69]),
           reads=["cnf:mneg"], writes=["cn:mnegbf"])
    p.emit("pe", lambda t: t.matmul(
        c.ps[6][:, 0:1], c.CN[0:1, 2:130], c.CN[0:1, 286:287], start=True, stop=True),
        reads=["cn:mnegbf", "cn:onesrow"], writes=[k_ps(6)])
    p.emit("act", lambda s: s.activation(c.CNF[:, 70:71], c.ps[6][:, 0:1], AF.Copy),
           reads=[k_ps(6)], writes=["cnf:mbias"])

    p.merge("a1")
    P_t = [c.AR1[:, i * TB:(i + 1) * TB] for i in range(4)]
    Ocol = c.AR1[:, 4 * TB:12 * TB].rearrange("p (u n) -> p u n", n=TB)
    nmask = {"full": 0, "l0re": 0, "alibi": 4, "window": 8}[kind]
    masks = []
    if nmask:
        mbase = 12 * TB
        p.emit("pool", lambda g: g.dma_start(
            out=c.AR1[:, mbase:mbase + nmask * TB], in_=c.dram[masks_name][:, :]),
            writes=["a1:masks"], dma="w")
        masks = [c.AR1[:, mbase + i * TB: mbase + (i + 1) * TB] for i in range(nmask)]
    slopes = alibi_slopes(H)
    if kind == "alibi":
        qa0 = 16 * TB
        qaug = c.AR1[0:3, qa0:qa0 + 8 * TB].rearrange("p (hh n) -> p hh n", n=TB)
        kaug = c.AR1[0:3, 24 * TB:24 * TB + 2048]
        p.emit("pool", lambda g: g.dma_start(out=kaug, in_=c.dram["alibi_kaug"][:, :]),
               writes=["a1:kaug"], dma="w")
        p.emit("dve", lambda v: v.tensor_scalar(
            c.CNF[0:1, 69:70], c.CNF[0:1, 67:68], -1.0, None, ALU.mult),
            reads=["cnf:msqrt"], writes=["cnf:m8"])

    SB_RING = (0, 1, 5, 6)
    AV_B = (2, 7)
    BC_B = (3, 4)
    ROWT = (c.ROWS[0:1, 0:512], c.ROWS[0:1, 512:1024])
    ROWK = ("rows:mu", "rows:tmp")
    BRECS = (c.BROWS[0:1, 512:1024], c.BROWS[0:1, 0:512])
    BRECK = ("brows:r", "brows:0")

    def pipeline(items):
        K = PIPE_K
        n = len(items)
        for i in range(n + K):
            if i < n:
                items[i][0]()
            if i >= K:
                items[i - K][1]()

    def noop():
        pass

    def mk_norm_items(up, avb, evac_fn):
        rt, rk = ROWT[up], ROWK[up]
        br, brk = BRECS[up], BRECK[up]
        bcb = BC_B[up]

        def n1():
            p.emit("dve", lambda v: v.tensor_scalar(
                rt, c.ps[avb][64:65, :], 1e-30, None, ALU.max),
                reads=[k_ps(avb)], writes=[rk])

            def recip(v):
                with c.nc.allow_low_precision(reason="softmax denominator is applied in bf16 (as before via copy)"):
                    return v.reciprocal(br, rt)
            p.emit("dve", recip, reads=[rk], writes=[brk])

        def n2():
            p.emit("pe", lambda t: t.matmul(
                c.ps[bcb][0:64, :], c.CN[0:1, 2:66], br, start=True, stop=True),
                reads=[brk, "cn:onesrow"], writes=[k_ps(bcb)])
            evac_fn(bcb)
        return [(noop, n1), (noop, n2)]

    def emit_outproj(ib):
        for dt in range(8):
            slot = dt % 2
            wkk = "ws:%d" % slot
            wv3 = c.WS[:, slot * 1024:(slot + 1) * 1024].rearrange(
                "p (kc n) -> p kc n", kc=8)
            p.emit("pool", lambda g, dt=dt, wv3=wv3: g.dma_start(
                out=wv3, in_=c.dram[wo][:, dt * 128:(dt + 1) * 128]
                .rearrange("(kc p) n -> p kc n", p=128)),
                writes=[wkk], dma="w")
            bank = 4 + dt % 2
            pk = k_ps(bank)

            def mmO(t, wv3=wv3, bank=bank):
                last = None
                for oc in range(8):
                    last = t.matmul(c.ps[bank][:, :], wv3[:, oc, :], Ocol[:, oc, :],
                                    start=(oc == 0), stop=(oc == 7))
                return last
            p.emit("pe", mmO,
                   reads=[wkk] + ["a1:O:%d" % g2 for g2 in range(4)] +
                         ["a1:O:%d" % (4 + g2) for g2 in range(4)],
                   writes=[pk])
            xsl = c.xT[:, dt * S + ib * TB: dt * S + (ib + 1) * TB]
            if bo_cols is None:
                p.emit("dve", lambda v, xsl=xsl, bank=bank: v.tensor_tensor(
                    xsl, xsl, c.ps[bank][:, :], ALU.add),
                    reads=[pk, "xT:%d" % dt], writes=["xT:%d" % dt])
            else:
                p.emit("dve", lambda v, xsl=xsl, bank=bank, dt=dt:
                       v.scalar_tensor_tensor(
                           xsl, c.ps[bank][:, :], c.CNF[:, 72 + dt:73 + dt], xsl,
                           ALU.add, ALU.add),
                       reads=[pk, "xT:%d" % dt, "cnf:bo"], writes=["xT:%d" % dt])

    if kind == "l0re":
        karrs = [c.AR1[0:64, 12 * TB + i * 2048:12 * TB + (i + 1) * 2048]
                 for i in range(2)]
        qarrs = [c.AR1[0:64, 20 * TB + i * 2048:20 * TB + (i + 1) * 2048]
                 for i in range(2)]
        ic = [0]
        for ib in range(NT):
            items = []
            for hh in range(4):
                h = 4 * ib + hh
                karr, qarr = karrs[h % 2], qarrs[h % 2]
                kak, qak = "a1:karr%d" % (h % 2), "a1:qarr%d" % (h % 2)

                def stage(h=h, karr=karr, qarr=qarr, kak=kak, qak=qak):
                    for par in range(2):
                        ksrc = k_ar[64 * par:64 * par + 64, :].rearrange(
                            "p (pt n) -> p pt n", pt=8)[:, :, h * 128:(h + 1) * 128]
                        kdst = karr.rearrange("p (pt two n) -> p pt two n",
                                              two=2, n=128)[:, :, par, :]
                        p.emit("dve", lambda v, ksrc=ksrc, kdst=kdst:
                               v.tensor_copy(kdst, ksrc),
                               reads=["a3:%d" % pt2 for pt2 in range(8)],
                               writes=[kak])
                        qsrc = q_ar[64 * par:64 * par + 64, :].rearrange(
                            "p (pt n) -> p pt n", pt=8)[:, :, h * 128:(h + 1) * 128]
                        qdst = qarr.rearrange("p (pt two n) -> p pt two n",
                                              two=2, n=128)[:, :, par, :]
                        p.emit("dve", lambda v, qsrc=qsrc, qdst=qdst:
                               v.tensor_copy(qdst, qsrc),
                               reads=["a2:%d" % pt2 for pt2 in range(8)],
                               writes=[qak])
                first_of_h = [True]
                for gqq in range(4):
                    half, oc0 = gqq // 2, (gqq % 2) * 4
                    up = gqq % 2
                    avb = AV_B[up]
                    for r in range(4):
                        gq = 4 * gqq + r
                        for gkq in range(4):
                            im = ic[0] % 4
                            sb_ = SB_RING[im]
                            Pt = P_t[im]
                            pkP = "a1:P%d" % im
                            ic[0] += 1

                            def s_fn(gq=gq, gkq=gkq, sb_=sb_, Pt=Pt, pkP=pkP,
                                     karr=karr, qarr=qarr, kak=kak, qak=qak,
                                     do_stage=first_of_h[0], stage=stage):
                                if do_stage:
                                    stage()
                                qsl = qarr[:, gq * 128:(gq + 1) * 128]

                                def mmS4(t):
                                    last = None
                                    for gr in range(4):
                                        gk = 4 * gkq + gr
                                        last = t.matmul(
                                            c.ps[sb_][:, gr * 128:(gr + 1) * 128],
                                            karr[:, gk * 128:(gk + 1) * 128],
                                            qsl, start=True, stop=True)
                                    return last
                                p.emit("pe", mmS4, reads=[kak, qak],
                                       writes=[k_ps(sb_)])
                                p.emit("act", lambda s: s.activation(
                                    Pt, c.ps[sb_][:, :], AF.Exp,
                                    bias=c.CNF[:, 70:71], scale=scale),
                                    reads=[k_ps(sb_), "cnf:mbias"], writes=[pkP])
                            first_of_h[0] = False

                            def a_fn(gq=gq, gkq=gkq, r=r, h=h, Pt=Pt, pkP=pkP,
                                     avb=avb):
                                def mmAV4(t):
                                    last = None
                                    for gr in range(4):
                                        gk = 4 * gkq + gr
                                        last = t.matmul(
                                            c.ps[avb][0:65, r * 128:(r + 1) * 128],
                                            vt[:, h, 65 * gk:65 * gk + 65],
                                            Pt[:, gr * 128:(gr + 1) * 128],
                                            start=(gkq == 0 and gk % 4 == 0),
                                            stop=(gkq == 3 and gk % 4 == 3))
                                    return last
                                p.emit("pe", mmAV4,
                                       reads=["a4:v:%d:%d" % (h, 2 * gkq),
                                              "a4:v:%d:%d" % (h, 2 * gkq + 1),
                                              pkP, "a4:ones"],
                                       writes=[k_ps(avb)])
                            items.append((s_fn, a_fn))

                    def mk_evac(hh=hh, half=half, oc0=oc0, avb=avb, up=gqq % 2):
                        def evac(bcb):
                            osl = Ocol[64 * half:64 * half + 64, oc0:oc0 + 4,
                                       hh * 128:(hh + 1) * 128]
                            okey = "a1:O:%d" % (half * 2 + oc0 // 4)
                            srcp = c.ps[avb][0:64, :].rearrange(
                                "p (r n) -> p r n", n=128)
                            srcr = c.ps[bcb][0:64, :].rearrange(
                                "p (r n) -> p r n", n=128)
                            p.emit("act", lambda s: s.activation(osl, srcp, AF.Copy),
                                   reads=[k_ps(avb)], writes=[okey])
                            p.emit("dve", lambda v: v.tensor_tensor(
                                osl, osl, srcr, ALU.mult),
                                reads=[okey, k_ps(bcb)], writes=[okey])
                        return evac
                    items.extend(mk_norm_items(gqq % 2, avb, mk_evac()))
            pipeline(items)
            emit_outproj(ib)
        return

    ic = [0]
    for ib in range(NT):
        if kind == "full":
            jts = list(range(16))
        elif kind == "alibi":
            jts = list(range(4 * ib + 4))
        else:
            jts = list(range(max(0, 4 * ib - 4), 4 * ib + 4))
        if kind == "alibi":
            p.emit("sp", lambda sy, ib=ib: sy.dma_start(
                out=c.ROWS[0:1, 1024:1536], in_=c.dram["iota_c"][ib:ib + 1, :]),
                writes=["rows:rstd"], dma="x")
        items = []
        for h in range(H):
            def qaug_build(h=h):
                hf = h // 8
                for h2 in range(8):
                    p.emit("dve", lambda v, h2=h2, hf=hf: v.tensor_scalar(
                        qaug[0:1, h2, :], c.ROWS[0:1, 1024:1536],
                        -slopes[8 * hf + h2],
                        c.CNF[0:1, 69:70], ALU.mult, ALU.add),
                        reads=["rows:rstd", "cnf:m8", "a1:qr1"],
                        writes=["a1:qs%d" % h2, "a1:qr0"])
                p.emit("pool", lambda g: g.dma_start(
                    out=qaug[1:2, :, :], in_=qaug[0:1, :, :]),
                    reads=["a1:qs%d" % h2 for h2 in range(8)] + ["a1:qr0"],
                    writes=["a1:qr1"], dma="qa")
                p.emit("pool", lambda g, hf=hf: g.dma_start(
                    out=qaug[0:1, :, :],
                    in_=c.dram["alibi_qrows"][8 * hf:8 * hf + 8, :]
                    .rearrange("(o hh) n -> o hh n", o=1)),
                    reads=["a1:qr1"] + ["a1:qs%d" % h2 for h2 in range(8)],
                    writes=["a1:qr0"], dma="qa")
                p.emit("pool", lambda g, hf=hf: g.dma_start(
                    out=qaug[2:3, :, :],
                    in_=c.dram["alibi_qrows"][8 * hf:8 * hf + 8, :]
                    .rearrange("(o hh) n -> o hh n", o=1)),
                    writes=["a1:qr2"], dma="qa")
            pt, hr = h // 2, h % 2
            up = h % 2
            avb = AV_B[up]
            qsl = q_ar[64 * hr:64 * hr + 64,
                       pt * S + ib * TB: pt * S + (ib + 1) * TB]
            for idx, jt in enumerate(jts):
                im = ic[0] % 4
                sb_ = SB_RING[im]
                Pt = P_t[im]
                pkP = "a1:P%d" % im
                ic[0] += 1
                ksl = k_ar[64 * hr:64 * hr + 64,
                           pt * S + jt * 128: pt * S + jt * 128 + 128]
                mi = None
                if kind == "alibi" and jt >= 4 * ib:
                    mi = jt - 4 * ib
                elif kind == "window":
                    mi = jt - (4 * ib - 4)
                if mi is not None and not (0 <= mi < nmask):
                    mi = None

                def s_fn(ksl=ksl, qsl=qsl, sb_=sb_, Pt=Pt, pkP=pkP, jt=jt, h=h,
                         mi=mi, build=(kind == "alibi" and h % 8 == 0 and idx == 0),
                         qaug_build=qaug_build):
                    if build:
                        qaug_build()
                    if kind == "alibi":
                        def mmS(t):
                            t.matmul(c.ps[sb_][:, :], ksl, qsl, start=True,
                                     stop=False)
                            return t.matmul(
                                c.ps[sb_][:, :], kaug[:, jt * 128:(jt + 1) * 128],
                                qaug[:, h % 8, :], start=False, stop=True)
                        rds = ["a3:%d" % (h // 2), "a2:%d" % (h // 2), "a1:kaug",
                               "a1:qr0", "a1:qr1", "a1:qr2"]
                    else:
                        def mmS(t):
                            return t.matmul(c.ps[sb_][:, :], ksl, qsl,
                                            start=True, stop=True)
                        rds = ["a3:%d" % (h // 2), "a2:%d" % (h // 2)]
                    p.emit("pe", mmS, reads=rds, writes=[k_ps(sb_)])
                    if kind == "alibi":
                        p.emit("act", lambda s: s.activation(
                            Pt, c.ps[sb_][:, :], AF.Exp, scale=scale),
                            reads=[k_ps(sb_)], writes=[pkP])
                    else:
                        p.emit("act", lambda s: s.activation(
                            Pt, c.ps[sb_][:, :], AF.Exp, bias=c.CNF[:, 70:71],
                            scale=scale),
                            reads=[k_ps(sb_), "cnf:mbias"], writes=[pkP])
                    if mi is not None:
                        meng = "pool" if (jt % 2 == 0) else "dve"
                        p.emit(meng, lambda g: g.tensor_tensor(
                            Pt, Pt, masks[mi], ALU.mult),
                            reads=[pkP, "a1:masks"], writes=[pkP])

                def a_fn(jt=jt, h=h, Pt=Pt, pkP=pkP, avb=avb,
                         first=(idx == 0), last=(idx == len(jts) - 1)):
                    def mmAV(t):
                        return t.matmul(c.ps[avb][0:65, :],
                                        vt[:, jt, 65 * h:65 * h + 65],
                                        Pt, start=first, stop=last)
                    p.emit("pe", mmAV,
                           reads=["a4:v:%d:%d" % (jt, h // 2), pkP, "a4:ones"],
                           writes=[k_ps(avb)])
                items.append((s_fn, a_fn))

            def mk_evac(pt=pt, hr=hr, avb=avb):
                def evac(bcb):
                    osl = Ocol[64 * hr:64 * hr + 64, pt, :]
                    okey = "a1:O:%d" % pt
                    p.emit("act", lambda s: s.activation(
                        osl, c.ps[avb][0:64, :], AF.Copy),
                        reads=[k_ps(avb)], writes=[okey])
                    p.emit("dve", lambda v, bcb=bcb: v.tensor_tensor(
                        osl, osl, c.ps[bcb][0:64, :], ALU.mult),
                        reads=[okey, k_ps(bcb)], writes=[okey])
                return evac
            items.extend(mk_norm_items(up, avb, mk_evac()))
        pipeline(items)
        emit_outproj(ib)



def emit_ffn0(c, uid):
    p = c.p
    p.merge("a1"); p.merge("a2"); p.merge("a3"); p.merge("a4"); p.merge("cnf")
    p.emit("sp", lambda sy: sy.dma_start(out=c.CNF[:, 0:112], in_=c.dram["f0cols"][:, :]),
           writes=["cnf:f0"], dma="x")
    lng, lnb = c.CNF[:, 0:8], c.CNF[:, 8:16]
    attnb, nextb = c.CNF[:, 16:24], c.CNF[:, 24:32]
    b1 = [c.CNF[:, 32:64], c.CNF[:, 64:96]]
    b2 = [c.CNF[:, 96:104], c.CNF[:, 104:112]]
    emit_layernorm(c, c.AR1, "a1")
    emit_layernorm(c, c.AR2, "a2", src=c.AR1, src_key="a1",
                   gains=(lng, lnb, "cnf:f0"))

    def attn_sink(ot, nb, bank, pk):
        dsl = c.AR3[:, ot * S + nb * TB: ot * S + (nb + 1) * TB]
        xnsl = c.AR1[:, ot * S + nb * TB: ot * S + (nb + 1) * TB]
        p.emit("dve", lambda v, dsl=dsl, xnsl=xnsl, bank=bank, ot=ot:
               v.scalar_tensor_tensor(
                   dsl, c.ps[bank][:, :], attnb[:, ot:ot + 1], xnsl,
                   ALU.add, ALU.add),
               reads=[pk, "cnf:f0", "a1:%d" % ot], writes=["a3:%d" % ot])
    emit_projA(c, c.dram["attn_w"], D, c.AR2, "a2", attn_sink)
    p.merge("a1")
    p.merge("a2")
    hbuf = c.AR2.rearrange("p (fc n) -> p fc n", n=TB)
    wgs = [c.AR4[:, 0:8192].rearrange("p (kc n) -> p kc n", kc=8),
           c.AR4[:, 8192:16384].rearrange("p (kc n) -> p kc n", kc=8)]
    for ee in range(2):
        wn1, wn2 = ("w1A", "w2A") if ee == 0 else ("w1B", "w2B")
        for tb in range(2):
            nb = ee * 2 + tb
            for fg in range(4):
                w1g = wgs[fg % 2]
                wk1 = "a4:g%d" % (fg % 2)
                p.emit("pool", lambda g, wn1=wn1, fg=fg, w1g=w1g: g.dma_start(
                    out=w1g, in_=c.dram[wn1][:, fg * 1024:(fg + 1) * 1024]
                    .rearrange("(kc p) n -> p kc n", p=128)),
                    writes=[wk1], dma="w")
                for ft in range(8):
                    bank = ft % 2
                    pk = k_ps(bank)

                    def mmh(t, ft=ft, bank=bank, nb=nb, w1g=w1g):
                        last = None
                        for kc in range(8):
                            last = t.matmul(
                                c.ps[bank][:, :], w1g[:, kc, ft * 128:(ft + 1) * 128],
                                c.AR3[:, kc * S + nb * TB: kc * S + (nb + 1) * TB],
                                start=(kc == 0), stop=(kc == 7))
                        return last
                    p.emit("pe", mmh,
                           reads=[wk1] + ["a3:%d" % kc for kc in range(8)],
                           writes=[pk])
                    fc = fg * 8 + ft
                    b1col = b1[ee][:, fc:fc + 1]
                    p.emit("act", lambda s, fc=fc, bank=bank, b1col=b1col:
                           s.activation(hbuf[:, fc, :], c.ps[bank][:, :],
                                        AF.Gelu, bias=b1col),
                           reads=[pk, "cnf:f0"], writes=["a2:h%d" % fc])
            for dh in range(2):
                for fg in range(4):
                    w2g = wgs[fg % 2]
                    wk2 = "a4:g%d" % (fg % 2)
                    p.emit("pool", lambda g, wn2=wn2, fg=fg, w2g=w2g: g.dma_start(
                        out=w2g, in_=c.dram[wn2][fg * 1024:(fg + 1) * 1024, :]
                        .rearrange("(kc p) n -> p kc n", p=128)),
                        writes=[wk2], dma="w")
                    for dq in range(4):
                        dt = dh * 4 + dq
                        bank = 2 + dq
                        pk = k_ps(bank)

                        def mmy(t, fg=fg, dt=dt, bank=bank, w2g=w2g,
                                first=(fg == 0), last=(fg == 3)):
                            lastm = None
                            for kc in range(8):
                                lastm = t.matmul(
                                    c.ps[bank][:, :],
                                    w2g[:, kc, dt * 128:(dt + 1) * 128],
                                    hbuf[:, fg * 8 + kc, :],
                                    start=(first and kc == 0),
                                    stop=(last and kc == 7))
                            return lastm
                        p.emit("pe", mmy,
                               reads=[wk2] + ["a2:h%d" % (fg * 8 + kc2)
                                              for kc2 in range(8)],
                               writes=[pk])
                        if fg == 3:
                            ysl = c.AR1[:, dt * S + nb * TB: dt * S + (nb + 1) * TB]
                            p.emit("dve", lambda v, ysl=ysl, bank=bank, dt=dt, ee=ee:
                                   v.tensor_scalar(ysl, c.ps[bank][:, :],
                                                   b2[ee][:, dt:dt + 1], None,
                                                   ALU.add),
                                   reads=[pk, "cnf:f0"], writes=["a1:y%d" % dt])

    def next_sink(ot, nb, bank, pk):
        xsl = c.xT[:, ot * S + nb * TB: ot * S + (nb + 1) * TB]
        gsl = c.SQ[:, (nb % 2) * TB:(nb % 2 + 1) * TB]
        p.emit("act", lambda s, bank=bank, ot=ot, gsl=gsl: s.activation(
            gsl, c.ps[bank][:, :], AF.Gelu, bias=nextb[:, ot:ot + 1]),
            reads=[pk, "cnf:f0"], writes=["sq:%d" % (nb % 2)])
        p.emit("dve", lambda v, xsl=xsl, gsl=gsl: v.tensor_tensor(
            xsl, xsl, gsl, ALU.add),
            reads=["sq:%d" % (nb % 2), "xT:%d" % ot], writes=["xT:%d" % ot])
    emit_projA(c, c.dram["next_w"], D, c.AR1, "a1:y", next_sink)


def emit_ssm(c, uid):
    p = c.p
    p.merge("a1"); p.merge("a2"); p.merge("a3"); p.merge("a4"); p.merge("cnf")
    p.emit("sp", lambda sy: sy.dma_start(out=c.CNF[:, 0:16], in_=c.dram["ssm_lncols"][:, :]),
           writes=["cnf:sln"], dma="x")
    lnw, lnb = c.CNF[:, 0:8], c.CNF[:, 8:16]
    abc = c.AR2[:, 0:3072].bitcast(F32)
    p.emit("sp", lambda sy: sy.dma_start(out=abc[:, 0:256], in_=c.dram["Alog_t"][:, :]),
           writes=["a2:alog"], dma="x")
    p.emit("sp", lambda sy: sy.dma_start(out=abc[:, 256:512], in_=c.dram["B_t"][:, :]),
           writes=["a2:bt"], dma="x")
    p.emit("sp", lambda sy: sy.dma_start(out=abc[:, 512:768], in_=c.dram["C_t"][:, :]),
           writes=["a2:ct"], dma="x")
    At = abc[:, 0:256].rearrange("p (it dd) -> p it dd", dd=16)
    Btv = abc[:, 256:512].rearrange("p (it dd) -> p it dd", dd=16)
    Ctv = abc[:, 512:768].rearrange("p (it dd) -> p it dd", dd=16)
    bc = abc[:, 768:1024].rearrange("p (it dd) -> p it dd", dd=16)
    hbc = c.CNF[:, 16:32]
    decc = c.CNF[:, 32:48]
    p.emit("dve", lambda v: v.tensor_tensor(bc[:, :, :], Btv[:, :, :], Ctv[:, :, :],
                                            ALU.mult),
           reads=["a2:bt", "a2:ct"], writes=["a2:bc"])
    for it in range(16):
        p.emit("dve", lambda v, it=it: v.reduce_sum(
            hbc[:, it:it + 1], bc[:, it, :], axis=AX.X),
            reads=["a2:bc"], writes=["cnf:hbc%d" % it])
        p.emit("dve", lambda v, it=it: v.tensor_scalar(
            hbc[:, it:it + 1], hbc[:, it:it + 1], 0.5, None, ALU.mult),
            reads=["cnf:hbc%d" % it], writes=["cnf:hbc%d" % it])
    p.emit("act", lambda s: s.activation(bc[:, :, :], At[:, :, :], AF.Exp),
           reads=["a2:alog"], writes=["a2:bc"])
    p.emit("dve", lambda v: v.tensor_scalar(
        bc[:, :, :], bc[:, :, :], -10.0, -0.01, ALU.max, ALU.min),
        reads=["a2:bc"], writes=["a2:bc"])
    p.emit("dve", lambda v: v.reduce_max(
        decc[:, 0:1], abc[:, 768:1024], axis=AX.X),
        reads=["a2:bc"], writes=["cnf:dec0"])
    p.emit("dve", lambda v: v.tensor_scalar(
        decc[:, 0:1], decc[:, 0:1], -1.0, None, ALU.mult),
        reads=["cnf:dec0"], writes=["cnf:dec0"])
    emit_layernorm(c, c.AR1, "a1")
    emit_layernorm(c, c.AR1, "a1g", src=c.AR1, src_key="a1",
                   gains=(lnw, lnb, "cnf:sln"), inplace=True)
    p.merge("a2")
    xps = c.AR2[:, 4096:6144]
    sgate = c.AR2[:, 6144:8192]
    decT = c.AR2[:, 8192:12288].bitcast(F32)
    sf = c.AR2[:, 12288:16384].bitcast(F32)
    sb2 = c.AR2[:, 0:4096].bitcast(F32)
    yv = c.AR3.rearrange("p (ic n) -> p ic n", ic=8)
    for half in range(2):
        for itl in range(8):
            it = half * 8 + itl
            for which, et in (("x", it), ("g", it + 16)):
                slot = 0 if which == "x" else 1
                wkk = "ws:%d" % slot
                wv3 = c.WS[:, slot * 1024:(slot + 1) * 1024].rearrange(
                    "p (kc n) -> p kc n", kc=8)
                p.emit("pool", lambda g, et=et, wv3=wv3: g.dma_start(
                    out=wv3, in_=c.dram["inpT"][:, et * 128:(et + 1) * 128]
                    .rearrange("(kc p) n -> p kc n", p=128)),
                    writes=[wkk], dma="w")
                bks = (0, 1, 2, 3) if which == "x" else (4, 5, 6, 7)

                def mmp(t, wv3=wv3, bks=bks):
                    last = None
                    for kc in range(8):
                        for nb in range(NT):
                            last = t.matmul(
                                c.ps[bks[nb]][:, :], wv3[:, kc, :],
                                c.AR1[:, kc * S + nb * TB: kc * S + (nb + 1) * TB],
                                start=(kc == 0), stop=(kc == 7))
                    return last
                p.emit("pe", mmp,
                       reads=[wkk] + ["a1g:%d" % kc for kc in range(8)],
                       writes=[k_ps(b) for b in bks])
                for nb in range(NT):
                    bank = bks[nb]
                    pk = k_ps(bank)
                    if which == "x":
                        dsl = xps[:, nb * TB:(nb + 1) * TB]
                        p.emit("act", lambda s, dsl=dsl, bank=bank: s.activation(
                            dsl, c.ps[bank][:, :], AF.Copy),
                            reads=[pk], writes=["a2:xps"])
                    else:
                        dsl = sgate[:, nb * TB:(nb + 1) * TB]
                        p.emit("act", lambda s, dsl=dsl, bank=bank: s.activation(
                            dsl, c.ps[bank][:, :], AF.Silu),
                            reads=[pk], writes=["a2:sg"])
            if half == 0 and itl == 0:
                p.emit("dve", lambda v: v.memset(decT[:, :], 1.0), writes=["a2:dec"])
                p.emit("dve", lambda v: v.tensor_scalar(
                    decT[:, :], decT[:, :], decc[:, 0:1], None, ALU.mult),
                    reads=["a2:dec", "cnf:dec0"], writes=["a2:dec"])
            p.emit("dve", lambda v: v.tensor_tensor_scan(
                sf[:, :], decT[:, :], xps[:, :], 0.0, ALU.mult, ALU.add),
                reads=["a2:dec", "a2:xps"], writes=["a2:sf"])
            p.emit("dve", lambda v: v.tensor_tensor_scan(
                sb2[:, ::-1], decT[:, :], xps[:, ::-1], 0.0, ALU.mult, ALU.add),
                reads=["a2:dec", "a2:xps"], writes=["a2:sb"])
            p.emit("pool", lambda g: g.tensor_tensor(
                sf[:, :], sf[:, :], sb2[:, :], ALU.add),
                reads=["a2:sf", "a2:sb"], writes=["a2:sf"])
            p.emit("dve", lambda v, it=it, itl=itl: v.scalar_tensor_tensor(
                yv[:, itl, :], sf[:, :], hbc[:, it:it + 1], sgate[:, :],
                ALU.mult, ALU.mult),
                reads=["a2:sf", "a2:sg", "cnf:hbc%d" % it],
                writes=["a3:y%d" % itl])
        for dt in range(8):
            slot = dt % 2
            wkk = "ws:%d" % slot
            wv3 = c.WS[:, slot * 1024:(slot + 1) * 1024].rearrange(
                "p (kc n) -> p kc n", kc=8)
            p.emit("pool", lambda g, dt=dt, half=half, wv3=wv3: g.dma_start(
                out=wv3, in_=c.dram["opT"][half * 1024:(half + 1) * 1024,
                                           dt * 128:(dt + 1) * 128]
                .rearrange("(kc p) n -> p kc n", p=128)),
                writes=[wkk], dma="w")
            bks = (0, 1, 2, 3) if dt % 2 == 0 else (4, 5, 6, 7)

            def mmo(t, wv3=wv3, bks=bks):
                last = None
                for kc in range(8):
                    for nb in range(NT):
                        last = t.matmul(
                            c.ps[bks[nb]][:, :], wv3[:, kc, :],
                            yv[:, kc, nb * TB:(nb + 1) * TB],
                            start=(kc == 0), stop=(kc == 7))
                return last
            p.emit("pe", mmo,
                   reads=[wkk] + ["a3:y%d" % kc for kc in range(8)],
                   writes=[k_ps(b) for b in bks])
            for nb in range(NT):
                bank = bks[nb]
                pk = k_ps(bank)
                xsl = c.xT[:, dt * S + nb * TB: dt * S + (nb + 1) * TB]
                p.emit("dve", lambda v, xsl=xsl, bank=bank: v.tensor_tensor(
                    xsl, xsl, c.ps[bank][:, :], ALU.add),
                    reads=[pk, "xT:%d" % dt], writes=["xT:%d" % dt])


def emit_moe(c, uid):
    p = c.p
    p.merge("a1"); p.merge("a2"); p.merge("a3"); p.merge("a4"); p.merge("cnf")
    emit_layernorm(c, c.AR1, "a1")
    xn = c.AR1
    p.emit("pool", lambda g: g.dma_start(
        out=c.CN[:, 258:386].rearrange("p (kc n) -> p kc n", kc=8),
        in_=c.dram["gate_w"].rearrange("(kc p) n -> p kc n", p=128)),
        writes=["cn:gw"], dma="w")
    p.emit("pool", lambda g: g.dma_start(
        out=c.CN[0:1, 386:402], in_=c.dram["gate_b"][:, :]),
        writes=["cn:gb"], dma="w")
    gw3 = c.CN[:, 258:386].rearrange("p (kc n) -> p kc n", kc=8)
    for tt in range(16):
        bank = tt % 2
        pk = k_ps(bank)

        def mmg(t, tt=tt, bank=bank):
            for kc in range(8):
                t.matmul(c.ps[bank][:, 0:16],
                         xn[:, kc * S + tt * 128: kc * S + (tt + 1) * 128],
                         gw3[:, kc, :], start=(kc == 0), stop=False)
            return t.matmul(c.ps[bank][:, 0:16], c.CN[0:1, 2:130],
                            c.CN[0:1, 386:402], start=False, stop=True)
        p.emit("pe", mmg,
               reads=["cn:gw", "cn:gb", "cn:onesrow"] +
                     ["a1:%d" % kc for kc in range(8)],
               writes=[pk])
        lg = c.CNF[:, 0:16]
        p.emit("act", lambda s, bank=bank: s.activation(lg, c.ps[bank][:, 0:16],
                                                        AF.Sigmoid),
               reads=[pk], writes=["cnf:lg"])
        m1 = c.CNF[:, 16:17]
        m2 = c.CNF[:, 17:18]
        tmp = c.CNF[:, 18:34]
        msk = c.CNF[:, 34:50]
        p.emit("dve", lambda v: v.reduce_max(m1, lg, axis=AX.X),
               reads=["cnf:lg"], writes=["cnf:m1"])
        p.emit("dve", lambda v: v.tensor_scalar(msk, lg, m1, None, ALU.is_ge),
               reads=["cnf:lg", "cnf:m1"], writes=["cnf:msk"])
        p.emit("dve", lambda v: v.scalar_tensor_tensor(
            tmp, msk, -1e30, lg, ALU.mult, ALU.add),
            reads=["cnf:msk", "cnf:lg"], writes=["cnf:tmp"])
        p.emit("dve", lambda v: v.reduce_max(m2, tmp, axis=AX.X),
               reads=["cnf:tmp"], writes=["cnf:m2"])
        p.emit("dve", lambda v: v.tensor_tensor(m1, m1, m2, ALU.add),
               reads=["cnf:m1", "cnf:m2"], writes=["cnf:sum12"])
        p.emit("dve", lambda v: v.reciprocal(m1, m1),
               reads=["cnf:sum12"], writes=["cnf:r12"])
        p.emit("dve", lambda v: v.tensor_scalar(msk, lg, m2, None, ALU.is_ge),
               reads=["cnf:lg", "cnf:m2"], writes=["cnf:msk"])
        p.emit("dve", lambda v: v.scalar_tensor_tensor(
            c.CN[:, 386 + 16:386 + 32], lg, m1, msk, ALU.mult, ALU.mult),
            reads=["cnf:lg", "cnf:r12", "cnf:msk"], writes=["cn:wtok"])
        pb = c.ps[2].bitcast(BF16)
        p.emit("pe", lambda t: t.transpose(
            pb[0:16, 0:128], c.CN[:, 402:418], c.CN[:, 130:258]),
            reads=["cn:wtok", "cn:ident"], writes=[k_ps(2)])
        p.emit("act", lambda s, tt=tt: s.activation(
            c.WS[0:16, tt * 128:(tt + 1) * 128], pb[0:16, 0:128], AF.Copy),
            reads=[k_ps(2)], writes=["ws:r%d" % tt])
    ydv = c.AR2.rearrange("p (dt n) -> p dt n", dt=8)
    p.emit("dve", lambda v: v.memset(c.AR2[:, :], 0.0), writes=["a2:yd"])
    wgv = c.AR4[:, 0:4096].rearrange("p (kc n) -> p kc n", kc=8)
    wuv = c.AR4[:, 4096:8192].rearrange("p (kc n) -> p kc n", kc=8)
    wdv2 = [c.AR4[:, 8192:12288].rearrange("p (kc n) -> p kc n", kc=4),
            c.AR4[:, 12288:16384].rearrange("p (kc n) -> p kc n", kc=4)]
    actv = c.AR3.rearrange("p (sl it n) -> p sl it n", sl=8, it=4)
    wb_sb = c.SQ[:, TB:2 * TB]
    for e in range(17):
        if e < 16:
            p.emit("pool", lambda g, e=e: g.dma_start(
                out=wgv, in_=c.dram["ewg"][e, :, :]
                .rearrange("(kc p) n -> p kc n", p=128)),
                writes=["a4:wg"], dma="w")
            p.emit("pool", lambda g, e=e: g.dma_start(
                out=wuv, in_=c.dram["ewu"][e, :, :]
                .rearrange("(kc p) n -> p kc n", p=128)),
                writes=["a4:wu"], dma="w")
            p.emit("pool", lambda g, e=e: g.dma_start(
                out=wdv2[e % 2], in_=c.dram["ewd"][e, :, :]
                .rearrange("(kc p) n -> p kc n", p=128)),
                writes=["a4:wd%d" % (e % 2)], dma="w")
        else:
            p.emit("pool", lambda g: g.dma_start(
                out=wgv, in_=c.dram["swg"].rearrange("(kc p) n -> p kc n", p=128)),
                writes=["a4:wg"], dma="w")
            p.emit("pool", lambda g: g.dma_start(
                out=wuv, in_=c.dram["swu"].rearrange("(kc p) n -> p kc n", p=128)),
                writes=["a4:wu"], dma="w")
            p.emit("pool", lambda g: g.dma_start(
                out=wdv2[e % 2], in_=c.dram["swd"].rearrange("(kc p) n -> p kc n", p=128)),
                writes=["a4:wd%d" % (e % 2)], dma="w")
        for tb in range(4):
            slv = actv[:, (e * 4 + tb) % 8, :, :]
            akey = "a3:s%d" % ((e * 4 + tb) % 8)
            if e < 16:
                p.emit("pool", lambda g, e=e, tb=tb: g.dma_start(
                    out=c.BROWS[0:1, 512:1024],
                    in_=c.WS[e:e + 1, tb * TB:(tb + 1) * TB]),
                    reads=["ws:r%d" % t2 for t2 in range(tb * 4, tb * 4 + 4)],
                    writes=["brows:r"], dma="qa")
                p.emit("pe", lambda t: t.matmul(
                    c.ps[4][:, :], c.CN[0:1, 2:130], c.BROWS[0:1, 512:1024],
                    start=True, stop=True),
                    reads=["brows:r", "cn:onesrow"], writes=[k_ps(4)])
                p.emit("act", lambda s: s.activation(
                    wb_sb, c.ps[4][:, :], AF.Copy),
                    reads=[k_ps(4)], writes=["sq:1"])
            for It in range(4):
                bhg, bhu = 0 + (It % 2), 2 + (It % 2)

                def mmhg(t, It=It, tb=tb, b=bhg):
                    last = None
                    for kc in range(8):
                        last = t.matmul(
                            c.ps[b][:, :], wgv[:, kc, It * 128:(It + 1) * 128],
                            xn[:, kc * S + tb * TB: kc * S + (tb + 1) * TB],
                            start=(kc == 0), stop=(kc == 7))
                    return last

                def mmhu(t, It=It, tb=tb, b=bhu):
                    last = None
                    for kc in range(8):
                        last = t.matmul(
                            c.ps[b][:, :], wuv[:, kc, It * 128:(It + 1) * 128],
                            xn[:, kc * S + tb * TB: kc * S + (tb + 1) * TB],
                            start=(kc == 0), stop=(kc == 7))
                    return last
                p.emit("pe", mmhg,
                       reads=["a4:wg"] + ["a1:%d" % kc for kc in range(8)],
                       writes=[k_ps(bhg)])
                p.emit("pe", mmhu,
                       reads=["a4:wu"] + ["a1:%d" % kc for kc in range(8)],
                       writes=[k_ps(bhu)])
                asl = slv[:, It, :]
                p.emit("act", lambda s, asl=asl, b=bhg: s.activation(
                    asl, c.ps[b][:, :], AF.Silu),
                    reads=[k_ps(bhg)], writes=[akey])
                p.emit("dve", lambda v, asl=asl, b=bhu: v.tensor_tensor(
                    asl, asl, c.ps[b][:, :], ALU.mult),
                    reads=[akey, k_ps(bhu)], writes=[akey])
                if e < 16:
                    p.emit("pool", lambda g, asl=asl: g.tensor_tensor(
                        asl, asl, wb_sb, ALU.mult),
                        reads=[akey, "sq:1"], writes=[akey])
            for dt in range(8):
                def mmyd(t, e=e, dt=dt, slv=slv):
                    last_m = None
                    for ic in range(4):
                        last_m = t.matmul(
                            c.ps[dt][:, :],
                            wdv2[e % 2][:, ic, dt * 128:(dt + 1) * 128],
                            slv[:, ic, :], start=(ic == 0), stop=(ic == 3))
                    return last_m
                p.emit("pe", mmyd, reads=["a4:wd%d" % (e % 2), akey],
                       writes=[k_ps(dt)])
                ysl = ydv[:, dt, tb * TB:(tb + 1) * TB]
                p.emit("dve", lambda v, ysl=ysl, dt=dt: v.tensor_tensor(
                    ysl, ysl, c.ps[dt][:, :], ALU.add),
                    reads=[k_ps(dt), "a2:yd"], writes=["a2:yd"])
    for dt in range(8):
        for nb in range(NT):
            xsl = c.xT[:, dt * S + nb * TB: dt * S + (nb + 1) * TB]
            xnsl = xn[:, dt * S + nb * TB: dt * S + (nb + 1) * TB]
            ysl = ydv[:, dt, nb * TB:(nb + 1) * TB]
            p.emit("dve", lambda v, xsl=xsl, ysl=ysl: v.tensor_tensor(
                xsl, xsl, ysl, ALU.add),
                reads=["a2:yd", "xT:%d" % dt], writes=["xT:%d" % dt])
            p.emit("dve", lambda v, xsl=xsl, xnsl=xnsl: v.tensor_tensor(
                xsl, xsl, xnsl, ALU.add),
                reads=["a1:%d" % dt, "xT:%d" % dt], writes=["xT:%d" % dt])


def cols8(vec):
    return np.ascontiguousarray(np.asarray(vec, dtype=np.float32).reshape(8, 128).T)


def cols32(vec):
    return np.ascontiguousarray(np.asarray(vec, dtype=np.float32).reshape(32, 128).T)


def it16(arr):
    a = np.asarray(arr, dtype=np.float32).reshape(16, 128, 16).transpose(1, 0, 2)
    return np.ascontiguousarray(a.reshape(128, 256))


def make_masks_alibi():
    m = np.zeros((128, 4 * TB), dtype=np.float32)
    j = np.arange(128)[:, None]
    i = np.arange(TB)[None, :]
    for mi in range(4):
        m[:, mi * TB:(mi + 1) * TB] = (j <= i - 128 * mi)
    return m


def make_masks_window():
    m = np.zeros((128, 8 * TB), dtype=np.float32)
    j = np.arange(128)[:, None]
    i = np.arange(TB)[None, :]
    for mi in range(8):
        d = 128 * mi - 512
        m[:, mi * TB:(mi + 1) * TB] = ((j + d <= i) & (j + d > i - 512))
    return m


def make_alibi_consts():
    slopes = alibi_slopes(H)
    kaug = np.zeros((3, 2048), dtype=np.float32)
    jj = np.arange(2048)
    kaug[0, :] = (jj % 128) - 64
    kaug[1, :] = 1.0
    kaug[2, :] = (jj // 128) * 128 + 64 - 1024
    qrows = np.zeros((H, TB), dtype=np.float32)
    for h in range(H):
        qrows[h, :] = slopes[h]
    iota = np.zeros((4, TB), dtype=np.float32)
    for ib in range(4):
        iota[ib, :] = np.arange(ib * TB, (ib + 1) * TB) - 1024.0
    return kaug, qrows, iota


def host_prep(inputs):
    g = {}
    f32 = lambda a: np.ascontiguousarray(np.asarray(a, dtype=np.float32))
    T = lambda a: np.ascontiguousarray(np.asarray(a, dtype=np.float32).T)
    g["wqT"] = T(inputs["L0_attn_a_wq"])
    g["wkT"] = T(inputs["L0_attn_a_wk"])
    g["wvT"] = T(inputs["L0_attn_a_wv"])
    woT = T(inputs["L0_attn_a_wo"])
    rr = np.arange(1024)
    gq = (rr // 128) % 8 + 8 * ((rr % 128) // 64)
    o_of_r = 64 * gq + (rr % 64)
    g["woTp"] = np.ascontiguousarray(woT[o_of_r, :])
    g["attn_w"] = f32(inputs["L0_ffn_attn_w"])
    g["next_w"] = f32(inputs["L0_ffn_next_w"])
    g["f0cols_parts"] = (inputs["L0_ffn_ln_g"], inputs["L0_ffn_ln_b"],
                         inputs["L0_ffn_attn_b"], inputs["L0_ffn_next_b"])
    g["inpT"] = T(inputs["L0_attn_b_in_proj_w"])
    g["opT"] = T(inputs["L0_attn_b_out_proj_w"])
    g["ssm_lncols"] = np.concatenate(
        [cols8(inputs["L0_attn_b_ln_w"]), cols8(inputs["L0_attn_b_ln_b"])], axis=1)
    g["Alog_t"] = it16(inputs["L0_attn_b_A_log"])
    g["B_t"] = it16(inputs["L0_attn_b_B_mat"])
    g["C_t"] = it16(inputs["L0_attn_b_C_mat"])
    g["l1wq"] = f32(inputs["L1_attn_a_Wq"])
    g["l1wk"] = f32(inputs["L1_attn_a_Wk"])
    g["l1wv"] = f32(inputs["L1_attn_a_Wv"])
    g["l1wo"] = f32(inputs["L1_attn_a_Wo"])
    g["l1aa_bo"] = cols8(inputs["L1_attn_a_bo"])
    g["l1bwqT"] = T(inputs["L1_attn_b_Wq"])
    g["l1bwkT"] = T(inputs["L1_attn_b_Wk"])
    g["l1bwvT"] = T(inputs["L1_attn_b_Wv"])
    g["l1bwoT"] = T(inputs["L1_attn_b_Wo"])
    g["gate_w"] = f32(inputs["L1_ffn_gate_w"])
    g["gate_b"] = f32(inputs["L1_ffn_gate_b"]).reshape(1, 16)
    g["swg"] = f32(inputs["L1_ffn_shared_wg"])
    g["swu"] = f32(inputs["L1_ffn_shared_wu"])
    g["swd"] = f32(inputs["L1_ffn_shared_wd"])
    g["ewg"] = f32(inputs["L1_ffn_experts_wg"])
    g["ewu"] = f32(inputs["L1_ffn_experts_wu"])
    g["ewd"] = f32(inputs["L1_ffn_experts_wd"])
    g["ident"] = np.eye(128, dtype=np.float32)
    g["masks_a"] = make_masks_alibi()
    g["masks_w"] = make_masks_window()
    kaug, qrows, iota = make_alibi_consts()
    g["alibi_kaug"] = kaug
    g["alibi_qrows"] = qrows
    g["iota_c"] = iota
    return g


def percore_f0(inputs, core):
    eA = (2 * core) % 8
    eB = eA + 1
    d = {}
    d["w1A"] = np.ascontiguousarray(np.asarray(inputs["L0_ffn_w1"][eA], dtype=np.float32))
    d["w1B"] = np.ascontiguousarray(np.asarray(inputs["L0_ffn_w1"][eB], dtype=np.float32))
    d["w2A"] = np.ascontiguousarray(np.asarray(inputs["L0_ffn_w2"][eA], dtype=np.float32))
    d["w2B"] = np.ascontiguousarray(np.asarray(inputs["L0_ffn_w2"][eB], dtype=np.float32))
    lng, lnb, attnb, nextb = [cols8(v) for v in (
        inputs["L0_ffn_ln_g"], inputs["L0_ffn_ln_b"],
        inputs["L0_ffn_attn_b"], inputs["L0_ffn_next_b"])]
    b1A = cols32(inputs["L0_ffn_b1"][eA])
    b1B = cols32(inputs["L0_ffn_b1"][eB])
    b2A = cols8(inputs["L0_ffn_b2"][eA])
    b2B = cols8(inputs["L0_ffn_b2"][eB])
    d["f0cols"] = np.ascontiguousarray(np.concatenate(
        [lng, lnb, attnb, nextb, b1A, b1B, b2A, b2B], axis=1))
    return d


SUBLAYERS = [
    ("l0aa", lambda c: emit_attention(c, "l0aa", "l0re", "wqT", "wkT", "wvT", "woTp"),
     ["wqT", "wkT", "wvT", "woTp", "ident"]),
    ("l0f1", lambda c: emit_ffn0(c, "l0f1"),
     ["attn_w", "next_w", "w1A", "w1B", "w2A", "w2B", "f0cols"]),
    ("l0ab", lambda c: emit_ssm(c, "l0ab"),
     ["inpT", "opT", "ssm_lncols", "Alog_t", "B_t", "C_t"]),
    ("l0f2", lambda c: emit_ffn0(c, "l0f2"),
     ["attn_w", "next_w", "w1A", "w1B", "w2A", "w2B", "f0cols"]),
    ("l1aa", lambda c: emit_attention(c, "l1aa", "alibi", "l1wq", "l1wk", "l1wv",
                                      "l1wo", bo_cols="l1aa_bo",
                                      masks_name="masks_a"),
     ["l1wq", "l1wk", "l1wv", "l1wo", "l1aa_bo", "masks_a", "alibi_kaug",
      "alibi_qrows", "iota_c", "ident"]),
    ("l1f1", lambda c: emit_moe(c, "l1f1"),
     ["gate_w", "gate_b", "swg", "swu", "swd", "ewg", "ewu", "ewd", "ident"]),
    ("l1ab", lambda c: emit_attention(c, "l1ab", "window", "l1bwqT", "l1bwkT",
                                      "l1bwvT", "l1bwoT", masks_name="masks_w"),
     ["l1bwqT", "l1bwkT", "l1bwvT", "l1bwoT", "masks_w", "ident"]),
    ("l1f2", lambda c: emit_moe(c, "l1f2"),
     ["gate_w", "gate_b", "swg", "swu", "swd", "ewg", "ewu", "ewd", "ident"]),
]


def build_sublayer_program(idx, shapes):
    name, fn, extras = SUBLAYERS[idx]
    nc = bass.Bass("TRN2", target_bir_lowering=False)
    es = contextlib.ExitStack()
    io = {"x_in": ([D, S], F32, "ExternalInput"),
          "x_out": ([D, S], F32, "ExternalOutput")}
    for k in extras:
        io[k] = (shapes[k], F32, "ExternalInput")
    with es:
        c = alloc_common(nc, es, io)
        emit_init_consts(c, need_ident=("ident" in extras))
        emit_load_xT(c)
        fn(c)
        emit_store_xT(c)
        emit_final_wait(c)
        c.p.finalize()
    return nc


_prog_cache = {}


def build_fused_program(shapes):
    nc = bass.Bass("TRN2", target_bir_lowering=False)
    es = contextlib.ExitStack()
    io = {"x_in": ([D, S], F32, "ExternalInput"),
          "x_out": ([D, S], F32, "ExternalOutput")}
    allex = set()
    for name, fn, extras in SUBLAYERS:
        allex.update(extras)
    for k in sorted(allex):
        io[k] = (shapes[k], F32, "ExternalInput")
    with es:
        c = alloc_common(nc, es, io)
        emit_init_consts(c, need_ident=True)
        emit_load_xT(c)
        for name, fn, extras in SUBLAYERS:
            fn(c)
        emit_store_xT(c)
        emit_final_wait(c)
        c.p.finalize()
    return nc


FUSED = True


def kernel(**inputs):
    g = hp = host_prep(inputs)
    x = np.asarray(inputs["L0_attn_a_x"], dtype=np.float32)
    percore = [percore_f0(inputs, cc) for cc in range(NCORES)]
    xT = [np.ascontiguousarray(x[cc].T) for cc in range(NCORES)]
    shapes = {k: list(v.shape) for k, v in g.items() if isinstance(v, np.ndarray)}
    shapes.update({k: list(v.shape) for k, v in percore[0].items()})
    _ = inputs["L0_ffn_gate_w"]
    if FUSED:
        if "fused" not in _prog_cache:
            _prog_cache["fused"] = build_fused_program(shapes)
        nc = _prog_cache["fused"]
        allex = set()
        for name, fn, extras in SUBLAYERS:
            allex.update(extras)
        in_maps = []
        for cc in range(NCORES):
            m = {"x_in": xT[cc]}
            for k in sorted(allex):
                m[k] = percore[cc][k] if k in percore[cc] else g[k]
            in_maps.append(m)
        res = run_bass_kernel_spmd(nc, in_maps, core_ids=list(range(NCORES)))
        xT = [np.asarray(res.results[cc]["x_out"]) for cc in range(NCORES)]
        out = np.stack([xT[cc].T for cc in range(NCORES)], axis=0)
        return np.ascontiguousarray(out.astype(np.float32))
    for idx, (name, fn, extras) in enumerate(SUBLAYERS):
        ck = (idx,)
        if ck not in _prog_cache:
            _prog_cache[ck] = build_sublayer_program(idx, shapes)
        nc = _prog_cache[ck]
        in_maps = []
        for cc in range(NCORES):
            m = {"x_in": xT[cc]}
            for k in extras:
                m[k] = percore[cc][k] if k in percore[cc] else g[k]
            in_maps.append(m)
        res = run_bass_kernel_spmd(nc, in_maps, core_ids=list(range(NCORES)))
        xT = [np.asarray(res.results[cc]["x_out"]) for cc in range(NCORES)]
    out = np.stack([xT[cc].T for cc in range(NCORES)], axis=0)
    return np.ascontiguousarray(out.astype(np.float32))
